# version 07 (final)
# baseline (speedup 1.0000x reference)
import math
from contextlib import ExitStack

import numpy as np
import ml_dtypes

import concourse.bass as bass
import concourse.mybir as mybir
from concourse.bass_utils import run_bass_kernel_spmd

F32 = mybir.dt.float32
I32 = mybir.dt.int32
BF16 = mybir.dt.bfloat16
ALU = mybir.AluOpType
AF = mybir.ActivationFunctionType
AX = mybir.AxisListType

T = 4096
D = 2048
TOWN = 2048
KVH = 4
NTILE = 4
IEXP = 1408
NIC = 11
SI = 2816
NSC = 22
NEXP = 8
SCALE = 1.0 / math.sqrt(128.0)
EPS = 1e-5
NEG = -1.0e30

TWO_PI = 2.0 * math.pi
_C1 = float(np.float32(math.floor(TWO_PI * 1024.0) / 1024.0))
_C2 = float(np.float32(TWO_PI - _C1))
_C3 = float(np.float32(TWO_PI - _C1 - float(np.float32(TWO_PI - _C1))))
_LC = math.log(10000.0) / 64.0
_L1 = float(np.float32(_LC))
_L2 = float(np.float32(_LC - _L1))
_L3 = float(np.float32(_LC - _L1 - float(np.float32(_LC - _L1))))
_H1 = float(np.float32(math.pi / 2))
_H2 = float(np.float32(math.pi / 2 - _H1))
_PI_LO = float(np.nextafter(np.float32(math.pi), np.float32(0.0)))


class Sem:
    __slots__ = ("h", "n")

    def __init__(self, handle):
        self.h = handle
        self.n = 0


DEBUG_TAPS = False
STRIP = 0


class Builder:
    def __init__(self):
        self.taps = {}
        self.nc = bass.Bass("TRN2", target_bir_lowering=False,
                            detect_race_conditions=False)
        self.q = {e: [] for e in
                  ("tensor", "vector", "scalar", "gpsimd", "sync")}
        self.waited = {e: {} for e in self.q}
        self.st = ExitStack()
        self.gate = []
        self.wslot = {}

    def op(self, eng, build, waits=(), inc=None, incn=1, gated=True):
        w = []
        src = list(waits) + (self.gate if gated else [])
        for s, c in src:
            if s is None or c <= 0:
                continue
            prev = self.waited[eng].get(id(s), -1)
            if c > prev:
                w.append((s.h, c))
                self.waited[eng][id(s)] = c
        if inc is not None:
            inc.n += incn
            mark = (inc, inc.n)
            ih, iv = inc.h, incn
        else:
            mark, ih, iv = None, None, 0

        def f(e, w=w, build=build, ih=ih, iv=iv):
            for h, c in w:
                e.wait_ge(h, c)
            if build is not None:
                inst = build(e)
                if ih is not None:
                    inst.then_inc(ih, iv)

        self.q[eng].append(f)
        return mark

    def close_scope(self, stk):
        stk.close()
        self.gate = ([(self.SP, self.SP.n), (self.SV, self.SV.n),
                      (self.SS, self.SS.n), (self.SG, self.SG.n)]
                     + [(s, s.n) for s in self.SDZ])

    def tap(self, name, src_ap, shape, dtype):
        if not DEBUG_TAPS:
            return
        if STRIP and name not in ("q00_raw", "wf15", "wh15", "wl15",
                                  "xtd15", "xtdl15"):
            return
        t = self.nc.dram_tensor("dbg_" + name, list(shape), dtype,
                                kind="ExternalOutput").ap()
        w = [(self.SP, self.SP.n), (self.SV, self.SV.n),
             (self.SS, self.SS.n), (self.SG, self.SG.n)]
        m = self.dma_out(t, src_ap, waits=w)
        self.gate = list(self.gate) + [m]

    def dma_out(self, out_ap, in_ap, waits=()):
        return self.dma("gpsimd", self.SDZ, out_ap, in_ap, waits=waits)

    def dma(self, queue, sem_, out_ap, in_ap, waits=(), slow=False):
        if sem_ is self.SDZ:
            pool, rk = self.SDZ, "store"
        elif queue == "sync":
            pool, rk = self.SDW, "sync"
        else:
            pool, rk = self.SDG, "gpsimd"
        k = self._dma_rr[rk] % len(pool)
        self._dma_rr[rk] += 1
        sem = pool[k]
        w = list(waits) + [(sem, sem.n)]
        return self.op(queue,
                       lambda e, o=out_ap, i=in_ap, slow=slow: e.dma_start(
                           out=o, in_=i, allow_slow_non_contiguous=slow),
                       waits=w, inc=sem, incn=16)

    def bw(self, i):
        return [self.bank_free[i]] if self.bank_free[i] else []

    def mm_group(self, bank_i, pairs, n_free, waits=(), psum=None):
        ps = psum if psum is not None else self.pb[bank_i][:, :n_free]
        w = list(waits) + self.bw(bank_i)
        n = len(pairs)
        mark = None
        for i, (l, r) in enumerate(pairs):
            mark = self.op(
                "tensor",
                lambda e, l=l, r=r, a=(i == 0), o=(i == n - 1), ps=ps:
                    e.matmul(ps, l, r, start=a, stop=o),
                waits=w if i == 0 else (),
                inc=self.SP if i == n - 1 else None)
        self.bank_done[bank_i] = mark
        return mark

    def tick(self):
        return self.op(
            "tensor",
            lambda e: e.matmul(self.pb[4][0:1, 300:301],
                               self.ones1[0:1, 0:1], self.ones1[0:1, 0:1],
                               start=True, stop=True),
            waits=[self.m_ones], inc=self.SP)

    def evac(self, eng, bank_i, build, waits=()):
        m = self.op(eng, build,
                    waits=list(waits) + [self.bank_done[bank_i]],
                    inc=self.ESEM[eng])
        self.bank_free[bank_i] = m
        return m

    def split(self, stage_ap, hi_ap, lo_ap, waits=(), lo_eng="vector"):
        hm = self.op("scalar", lambda e: e.copy(hi_ap, stage_ap),
                     waits=waits, inc=self.SS)
        lm = self.op(lo_eng,
                     lambda e: e.scalar_tensor_tensor(
                         lo_ap, stage_ap, 1.0, hi_ap, ALU.mult, ALU.subtract),
                     waits=[hm], inc=self.ESEM[lo_eng])
        return hm, lm

    def build(self):
        nc, st = self.nc, self.st

        def din(name, shape, dtype=F32):
            return nc.dram_tensor(name, list(shape), dtype,
                                  kind="ExternalInput").ap()

        self.x_all = din("x_all", [T, D])
        self.x_own = din("x_own", [TOWN, D])
        self.pos_own = din("pos_own", [1, TOWN])
        self.pos_all = din("pos_all", [1, T])
        jmod_d = din("jmod", [128, 1])
        maskAB_d = din("maskAB", [128, 256])
        identF_d = din("identF", [128, 128])
        identB_d = din("identB", [128, 128], BF16)
        Rt_d = din("Rt", [128, 128])
        self.wq_w = din("wq_w", [D, D])
        wq_b = din("wq_b", [D])
        self.wk_w = din("wk_w", [D, 512])
        wk_b = din("wk_b", [512])
        self.wv_w = din("wv_w", [D, 512])
        wv_b = din("wv_b", [512])
        self.o_w = din("o_w", [D, D])
        self.o_b = din("o_b", [D])
        router_wT = din("router_wT", [D, 8])
        self.gate_w = din("gate_w", [NEXP, D, IEXP])
        self.up_w = din("up_w", [NEXP, D, IEXP])
        self.down_w = din("down_w", [NEXP, IEXP, D])
        self.s_gate_w = din("s_gate_w", [D, SI])
        s_gate_b = din("s_gate_b", [SI])
        self.s_up_w = din("s_up_w", [D, SI])
        s_up_b = din("s_up_b", [SI])
        self.s_down_w = din("s_down_w", [SI, D])
        self.s_down_b = din("s_down_b", [D])
        self.z = nc.dram_tensor("z", [TOWN, D], F32,
                                kind="ExternalOutput").ap()

        self.SDW = [Sem(st.enter_context(nc.semaphore(f"sdw{i}")))
                    for i in range(6)]
        self.SDG = [Sem(st.enter_context(nc.semaphore(f"sdg{i}")))
                    for i in range(4)]
        self.SDZ = [Sem(st.enter_context(nc.semaphore(f"sdz{i}")))
                    for i in range(4)]
        self._dma_rr = {"sync": 0, "gpsimd": 0, "store": 0}
        self.SP = Sem(st.enter_context(nc.semaphore("sp")))
        self.SV = Sem(st.enter_context(nc.semaphore("sv")))
        self.SS = Sem(st.enter_context(nc.semaphore("ss")))
        self.SG = Sem(st.enter_context(nc.semaphore("sg")))
        self.SZ = Sem(st.enter_context(nc.semaphore("sz")))
        self.ESEM = {"vector": self.SV, "scalar": self.SS,
                     "gpsimd": self.SG}

        self.pb = [st.enter_context(
            nc.psum_tensor(f"pb{i}", [128, 512], F32)) for i in range(6)]
        self.pt = [st.enter_context(
            nc.psum_tensor(f"pt{i}", [128, 1024], BF16)) for i in range(2)]
        self.bank_free = [None] * 8
        self.bank_done = [None] * 8

        def psb(name, free, dtype):
            return st.enter_context(nc.sbuf_tensor(name, [128, free], dtype))

        self.kvblob = psb("kvblob", 65536, BF16)
        self.ones1 = psb("ones1", 128, F32)
        self.identF_s = psb("identF_s", 128, F32)
        self.identB_s = psb("identB_s", 128, BF16)
        self.Rt_s = psb("Rt_s", 128, F32)
        self.maskAB_s = psb("maskAB_s", 256, F32)
        jmod_s = psb("jmod_s", 1, F32)
        self.invfreq = psb("invfreq", 1, F32)
        self.wqb_s = psb("wqb_s", 16, F32)
        self.wv_b = wv_b
        self.wkb_s = psb("wkb_s", 4, F32)
        self.sgb_s = psb("sgb_s", NSC, F32)
        self.sub_s = psb("sub_s", NSC, F32)
        rwT_f = psb("rwT_f", 128, F32)
        self.rwT_hi = psb("rwT_hi", 128, BF16)
        self.rwT_lo = psb("rwT_lo", 128, BF16)
        self.c_tile = psb("c_tile", 32, F32)
        self.stats = psb("stats", 16, F32)
        self.maxbuf = psb("maxbuf", 8, F32)
        self.rowsum = psb("rowsum", 1, F32)
        self.rinv = psb("rinv", 1, F32)
        self.trigi = psb("trigi", 256, I32)
        self.epsb = psb("epsb", 1, F32)
        self._logit = psb("logit", 8, F32)
        self._l2 = psb("l2b", 8, F32)
        self._sel = psb("selb", 8, F32)
        self._eb = psb("eb", 8, F32)

        SDW, SP, SV, SS, SG = self.SDW, self.SP, self.SV, self.SS, self.SG

        self.m_ones = self.op(
            "gpsimd", lambda e: e.memset(self.ones1[0:1, :], 1.0), inc=SG)
        self.m_eps = self.op(
            "gpsimd", lambda e: e.memset(self.epsb[:, :], EPS), inc=SG)
        self.b15 = psb("b15", 1, F32)
        self.m_b15 = self.op(
            "gpsimd", lambda e: e.memset(self.b15[:, :], 1.5), inc=SG)
        self.cm_id = self.dma("sync", SDW, self.identF_s[:, :], identF_d)
        self.cm_idb = self.dma("sync", SDW, self.identB_s[:, :], identB_d)
        self.cm_rt = self.dma("sync", SDW, self.Rt_s[:, :], Rt_d)
        self.cm_msk = self.dma("sync", SDW, self.maskAB_s[:, :], maskAB_d)
        cm_jm = self.dma("sync", SDW, jmod_s[:, :], jmod_d)
        self.cm_qb = self.dma("sync", SDW, self.wqb_s[:, :],
                              wq_b.rearrange("(c p) -> p c", p=128),
                              slow=True)
        self.cm_kb = self.dma("sync", SDW, self.wkb_s[:, :],
                              wk_b.rearrange("(c p) -> p c", p=128),
                              slow=True)
        self.cm_sgb = self.dma("sync", SDW, self.sgb_s[:, :],
                               s_gate_b.rearrange("(c p) -> p c", p=128),
                               slow=True)
        self.cm_sub = self.dma("sync", SDW, self.sub_s[:, :],
                               s_up_b.rearrange("(c p) -> p c", p=128),
                               slow=True)
        cm_rw = self.dma("sync", SDW,
                         rwT_f[:, :].rearrange("p (c e) -> p c e", c=16),
                         router_wT.rearrange("(c p) e -> p c e", p=128))
        self.m_rw = self.split(rwT_f[:, :], self.rwT_hi[:, :],
                               self.rwT_lo[:, :], waits=[cm_rw],
                               lo_eng="vector")[1]
        mI = self.op("vector", lambda e: e.tensor_single_scalar(
            self.invfreq[:, :], jmod_s[:, :], -_L1, ALU.mult),
            waits=[cm_jm], inc=SV)
        for L in (_L2, _L3):
            mI = self.op("vector", lambda e, L=L: e.scalar_tensor_tensor(
                self.invfreq[:, :], jmod_s[:, :], -L, self.invfreq[:, :],
                ALU.mult, ALU.add), inc=SV)
        self.m_invf = self.op("scalar", lambda e: e.activation(
            self.invfreq[:, :], self.invfreq[:, :], AF.Exp),
            waits=[mI], inc=SS)
        self.last_exp = None

        for tile in range(NTILE):
            self.emit_tile(tile)

        self.op("gpsimd", None, waits=[(s, s.n) for s in self.SDZ],
                gated=False)

        with nc.Block() as block:
            @block.sync
            def _(eng):
                for f in self.q["sync"]:
                    f(eng)

            @block.tensor
            def _(eng):
                for f in self.q["tensor"]:
                    f(eng)

            @block.vector
            def _(eng):
                for f in self.q["vector"]:
                    f(eng)

            @block.scalar
            def _(eng):
                for f in self.q["scalar"]:
                    f(eng)

            @block.gpsimd
            def _(eng):
                for f in self.q["gpsimd"]:
                    f(eng)

        st.close()
        return nc

    def K_ap(self, limb, unit, kvh, tok0, ntok):
        off = unit * 8192 + limb * 2048 + kvh * 512 + tok0
        return self.kvblob[:, off:off + ntok]

    def V_ap(self, limb, unit, blk, vf0, nvf):
        off = unit * 8192 + 4096 + limb * 2048 + blk * 512 + vf0
        return self.kvblob[:, off:off + nvf]

    def yT_ap(self, limb, tile, chunk, tok0, ntok):
        u = (2 * tile + 2 + limb) % 8
        off = u * 8192 + chunk * 512 + tok0
        return self.kvblob[:, off:off + ntok]

    def gen_tables(self, bufs, pos_dram, tok0, ntok):
        cosb, sinb, tg1, tg2, posb = bufs
        SV, SS = self.SV, self.SS
        pm = self.dma("gpsimd", self.SDG, posb[0:1, :ntok],
                      pos_dram[0:1, tok0:tok0 + ntok],
                      waits=list(self.wslot.get(("posb", id(posb)), ())))
        mtb = self.mm_group(5, [(self.ones1[0:1, :], posb[0:1, :ntok])], ntok,
                            waits=[pm, self.m_ones])
        self.wslot[("posb", id(posb))] = [mtb]
        self.evac("vector", 5, lambda e: e.tensor_scalar(
            tg1[:, :ntok], self.pb[5][:, :ntok], self.invfreq[:, 0:1], None,
            ALU.mult), waits=[self.m_invf])
        self.op("vector", lambda e: e.tensor_single_scalar(
            tg2[:, :ntok], tg1[:, :ntok], 1.0 / TWO_PI, ALU.mult), inc=SV)
        self.op("vector", lambda e: e.tensor_copy(
            self.trigi[:, :ntok], tg2[:, :ntok]), inc=SV)
        self.op("vector", lambda e: e.tensor_copy(
            tg2[:, :ntok], self.trigi[:, :ntok]), inc=SV)
        self.op("vector", lambda e: e.scalar_tensor_tensor(
            cosb[:, :ntok], tg2[:, :ntok], -_C1, tg1[:, :ntok],
            ALU.mult, ALU.add), inc=SV)
        for c in (_C2, _C3):
            self.op("vector", lambda e, c=c: e.scalar_tensor_tensor(
                cosb[:, :ntok], tg2[:, :ntok], -c, cosb[:, :ntok],
                ALU.mult, ALU.add), inc=SV)

        def wrap(dst):
            self.op("vector", lambda e: e.tensor_single_scalar(
                tg1[:, :ntok], dst[:, :ntok], math.pi, ALU.is_gt), inc=SV)
            for c in (_C1, _C2):
                self.op("vector", lambda e, c=c: e.scalar_tensor_tensor(
                    dst[:, :ntok], tg1[:, :ntok], -c, dst[:, :ntok],
                    ALU.mult, ALU.add), inc=SV)
            self.op("vector", lambda e: e.tensor_single_scalar(
                tg1[:, :ntok], dst[:, :ntok], -math.pi, ALU.is_lt), inc=SV)
            for c in (_C1, _C2):
                self.op("vector", lambda e, c=c: e.scalar_tensor_tensor(
                    dst[:, :ntok], tg1[:, :ntok], c, dst[:, :ntok],
                    ALU.mult, ALU.add), inc=SV)
            self.op("vector", lambda e: e.tensor_scalar_min(
                dst[:, :ntok], dst[:, :ntok], _PI_LO), inc=SV)
            return self.op("vector", lambda e: e.tensor_scalar_max(
                dst[:, :ntok], dst[:, :ntok], -_PI_LO), inc=SV)

        self.op("vector", lambda e: e.tensor_copy(
            sinb[:, :ntok], cosb[:, :ntok]), inc=SV)
        mw = wrap(sinb)
        msin = self.op("scalar", lambda e: e.activation(
            sinb[:, :ntok], sinb[:, :ntok], AF.Sin), waits=[mw], inc=SS)
        for c in (_H1, _H2):
            self.op("vector", lambda e, c=c: e.tensor_scalar_add(
                cosb[:, :ntok], cosb[:, :ntok], c), inc=SV)
        mw = wrap(cosb)
        mcos = self.op("scalar", lambda e: e.activation(
            cosb[:, :ntok], cosb[:, :ntok], AF.Sin), waits=[mw], inc=SS)
        return mcos, msin

    def build_xT(self, xthi, xtlo, xstage, x_dram, tok0, nblk):
        TRB = (3, 4)
        W = nblk * 128
        lm = None
        prev_tr = None
        for b2 in range(nblk):
            for g0 in range(0, 16, 4):
                xm = self.dma(
                    "sync", self.SDW, xstage[:, :],
                    x_dram[tok0 + b2 * 128:tok0 + b2 * 128 + 128,
                           g0 * 128:(g0 + 4) * 128],
                    waits=[prev_tr] if prev_tr else ())
                bi = TRB[(g0 // 4) % 2]
                w = [xm, self.cm_id] + self.bw(bi)
                mk = None
                for i in range(4):
                    mk = self.op(
                        "tensor",
                        lambda e, i=i, bi=bi:
                            e.transpose(self.pb[bi][:, i * 128:(i + 1) * 128],
                                        xstage[:, i * 128:(i + 1) * 128],
                                        self.identF_s[:, :]),
                        waits=w if i == 0 else (),
                        inc=self.SP if i == 3 else None)
                self.bank_done[bi] = mk
                prev_tr = mk

                def dst(t, g0=g0, W=W, b2=b2):
                    a = t[:, :].rearrange("p (c r) -> p c r", c=16)
                    return a[:, g0:g0 + 4, b2 * 128:b2 * 128 + 128]

                hm = self.op("scalar",
                             lambda e, g0=g0, bi=bi, b2=b2:
                                 e.copy(dst(xthi, g0, W, b2),
                                        self.pb[bi][:, :512]),
                             waits=[mk], inc=self.SS)
                lm = self.op("vector",
                             lambda e, g0=g0, bi=bi, b2=b2:
                                 e.scalar_tensor_tensor(
                                     dst(xtlo, g0, W, b2),
                                     self.pb[bi][:, :512], 1.0,
                                     dst(xthi, g0, W, b2),
                                     ALU.mult, ALU.subtract),
                             waits=[mk, hm], inc=self.SV)
                self.bank_free[bi] = lm
        return lm

    def rope_split(self, bank_i, ntok, bias_ap, dst_hi, dst_lo, kt, tbl,
                   tapname=None):
        kt1, kt2, kt3 = kt
        mcos, msin, cosb, sinb = tbl
        prev = self.wslot.get(("rope_ml", id(kt1)), ())
        m_ev = self.evac("scalar", bank_i,
                         lambda e: e.activation(
                             kt1[:, :ntok], self.pb[bank_i][:, :ntok],
                             AF.Identity, bias=bias_ap),
                         waits=[self.cm_qb, self.cm_kb] + list(prev))
        if tapname:
            self.tap(tapname + "_raw", kt1[:, :ntok], [128, ntok], F32)
        if STRIP >= 2:
            return m_ev
        if tapname:
            self.tap(tapname + "_cos", cosb[:, :ntok], [128, ntok], F32)
            self.tap(tapname + "_sin", sinb[:, :ntok], [128, ntok], F32)
        self.mm_group(4, [(self.Rt_s[:, :], kt1[:, :ntok])], ntok,
                      waits=[m_ev, self.cm_rt])
        mt1 = self.op("vector", lambda e: e.tensor_mul(
            kt2[:, :ntok], kt1[:, :ntok], cosb[:, :ntok]),
            waits=[m_ev, mcos, msin], inc=self.SV)
        self.evac("vector", 4, lambda e: e.tensor_mul(
            kt3[:, :ntok], self.pb[4][:, :ntok], sinb[:, :ntok]),
            waits=[mt1])
        mqr = self.op("vector", lambda e: e.tensor_add(
            kt1[:, :ntok], kt2[:, :ntok], kt3[:, :ntok]), inc=self.SV)
        mh = self.op("scalar", lambda e: e.copy(dst_hi, kt1[:, :ntok]),
                     waits=[mqr], inc=self.SS)
        ml = self.op("vector", lambda e: e.scalar_tensor_tensor(
            dst_lo, kt1[:, :ntok], 1.0, dst_hi, ALU.mult, ALU.subtract),
            waits=[mh], inc=self.SV)
        self.wslot[("rope_ml", id(kt1))] = [ml, mh]
        return ml

    def wstream(self, key, wf, wh, wl, src_fn, consume_fn, nd=16):
        for d in range(nd):
            s = d % 2
            sl = slice(s * 512, (s + 1) * 512)
            dm = self.dma("sync", self.SDW, wf[:, :], src_fn(d),
                          waits=self.wslot.get((key, "f"), ()))
            hm = self.op("scalar",
                         lambda e, sl=sl: e.copy(wh[:, sl], wf[:, :]),
                         waits=[dm] + list(self.wslot.get((key, "hl", s), ())),
                         inc=self.SS)
            lm = self.op("vector",
                         lambda e, sl=sl: e.scalar_tensor_tensor(
                             wl[:, sl], wf[:, :], 1.0, wh[:, sl],
                             ALU.mult, ALU.subtract),
                         waits=[hm], inc=self.SV)
            self.wslot[(key, "f")] = [lm]
            consume_fn(d, s, hm, lm)
            tm = self.tick()
            self.wslot[(key, "hl", s)] = [tm]

    def emit_tile(self, tile):
        nc = self.nc
        SP, SV, SS, SG = self.SP, self.SV, self.SS, self.SG
        tstk = ExitStack()
        acc = tstk.enter_context(
            nc.sbuf_tensor(f"acc{tile}", [128, 4 * 2048], F32))
        acc_mark = [None]

        for u in ([] if STRIP else (2 * tile, 2 * tile + 1)):
            bstk = ExitStack()
            nm = f"{tile}_{u}"
            xthi = bstk.enter_context(
                nc.sbuf_tensor(f"bxh{nm}", [128, 16 * 256], BF16))
            xtlo = bstk.enter_context(
                nc.sbuf_tensor(f"bxl{nm}", [128, 16 * 256], BF16))
            xstage = bstk.enter_context(
                nc.sbuf_tensor(f"bxs{nm}", [128, 512], F32))
            tbufs = [bstk.enter_context(
                nc.sbuf_tensor(f"bt{i}{nm}", [128, 256], F32))
                for i in range(4)]
            tbufs = tbufs + [tbufs[3]]
            kt1 = bstk.enter_context(
                nc.sbuf_tensor(f"bk0{nm}", [128, 256], F32))
            ktb = [kt1, tbufs[2], tbufs[3]]
            wf = bstk.enter_context(
                nc.sbuf_tensor(f"bwf{nm}", [128, 512], F32))
            wh = bstk.enter_context(
                nc.sbuf_tensor(f"bwh{nm}", [128, 1024], BF16))
            wl = bstk.enter_context(
                nc.sbuf_tensor(f"bwl{nm}", [128, 1024], BF16))
            wvb_row = bstk.enter_context(
                nc.sbuf_tensor(f"bvb{nm}", [128, 512], F32))
            cm_vb = self.dma("sync", self.SDW, wvb_row[0:1, :],
                             self.wv_b.rearrange("(a f) -> a f", a=1))
            for half in range(2):
                tok0 = u * 512 + half * 256
                mcos, msin = self.gen_tables(tbufs, self.pos_all, tok0, 256)
                mxt = self.build_xT(xthi, xtlo, xstage, self.x_all, tok0, 2)

                started = [False] * KVH

                def k_consume(d, s, hm, lm, started=started, mxt=mxt):
                    for kvh in range(KVH):
                        bi = kvh
                        lh = wh[:, s * 512 + kvh * 128:
                                s * 512 + kvh * 128 + 128]
                        ll = wl[:, s * 512 + kvh * 128:
                                s * 512 + kvh * 128 + 128]
                        rh = xthi[:, d * 256:(d + 1) * 256]
                        rl = xtlo[:, d * 256:(d + 1) * 256]
                        pairs = [(lh, rh), (lh, rl), (ll, rh)]
                        for i, (l, r) in enumerate(pairs):
                            first = not started[kvh] and i == 0
                            last = d == 15 and i == 2
                            w = ([hm, lm, mxt] + self.bw(bi)) if first else \
                                ([hm, lm] if i == 0 else ())
                            mk = self.op(
                                "tensor",
                                lambda e, l=l, r=r, a=first, o=last, bi=bi:
                                    e.matmul(self.pb[bi][:, :256], l, r,
                                             start=a, stop=o),
                                waits=w, inc=SP if last else None)
                            if last:
                                self.bank_done[bi] = mk
                        started[kvh] = True

                self.wstream((id(wf), "kv"), wf, wh, wl,
                             lambda d: self.wk_w[d * 128:(d + 1) * 128, :],
                             k_consume)
                for kvh in range(KVH):
                    mlr = self.rope_split(
                        kvh, 256, self.wkb_s[:, kvh:kvh + 1],
                        self.K_ap(0, u, kvh, half * 256, 256),
                        self.K_ap(1, u, kvh, half * 256, 256),
                        ktb, (mcos, msin, tbufs[0], tbufs[1]))
                self.wslot[("posb", id(tbufs[4]))] = [mlr]

                vstart = [False, False]

                def v_consume(d, s, hm, lm, vstart=vstart, mxt=mxt,
                              wvb_row=wvb_row, cm_vb=cm_vb):
                    for b2 in range(2):
                        bi = b2
                        lh = xthi[:, d * 256 + b2 * 128:
                                  d * 256 + b2 * 128 + 128]
                        ll = xtlo[:, d * 256 + b2 * 128:
                                  d * 256 + b2 * 128 + 128]
                        rh = wh[:, s * 512:(s + 1) * 512]
                        rl = wl[:, s * 512:(s + 1) * 512]
                        if not vstart[b2]:
                            self.op(
                                "tensor",
                                lambda e, bi=bi, wvb_row=wvb_row: e.matmul(
                                    self.pb[bi][:, :512], self.ones1[0:1, :],
                                    wvb_row[0:1, :], start=True,
                                    stop=False),
                                waits=[cm_vb, self.m_ones, hm, lm, mxt]
                                + self.bw(bi))
                            vstart[b2] = True
                        pairs = [(lh, rh), (lh, rl), (ll, rh)]
                        for i, (l, r) in enumerate(pairs):
                            last = d == 15 and i == 2
                            mk = self.op(
                                "tensor",
                                lambda e, l=l, r=r, o=last, bi=bi:
                                    e.matmul(self.pb[bi][:, :512], l, r,
                                             start=False, stop=o),
                                waits=[hm, lm] if i == 0 else (),
                                inc=SP if last else None)
                            if last:
                                self.bank_done[bi] = mk

                self.wstream((id(wf), "kv"), wf, wh, wl,
                             lambda d: self.wv_w[d * 128:(d + 1) * 128, :],
                             v_consume)
                for b2 in range(2):
                    blk = half * 2 + b2
                    mk = self.bank_done[b2]
                    hm2 = self.op("scalar",
                                  lambda e, b2=b2, u=u, blk=blk: e.copy(
                                      self.V_ap(0, u, blk, 0, 512),
                                      self.pb[b2][:, :512]),
                                  waits=[mk], inc=SS)
                    lm2 = self.op("vector",
                                  lambda e, b2=b2, u=u, blk=blk:
                                      e.scalar_tensor_tensor(
                                          self.V_ap(1, u, blk, 0, 512),
                                          self.pb[b2][:, :512], 1.0,
                                          self.V_ap(0, u, blk, 0, 512),
                                          ALU.mult, ALU.subtract),
                                  waits=[hm2], inc=SV)
                    self.bank_free[b2] = lm2
            self.close_scope(bstk)
        if tile == 0 and not STRIP:
            self.tap("kv01", self.kvblob[:, :16384], [128, 16384], BF16)
        if tile == 1 and not STRIP:
            self.tap("kv23", self.kvblob[:, 16384:32768], [128, 16384], BF16)

        for kvg in range(1 if STRIP else KVH):
            kstk = ExitStack()
            nm = f"{tile}_{kvg}"
            ao_hi = kstk.enter_context(
                nc.sbuf_tensor(f"aoh{nm}", [128, 16 * 128], BF16))
            ao_lo = kstk.enter_context(
                nc.sbuf_tensor(f"aol{nm}", [128, 16 * 128], BF16))
            qascope = ExitStack()
            QT_hi = qascope.enter_context(
                nc.sbuf_tensor(f"qh{nm}", [128, 4 * 512], BF16))
            QT_lo = qascope.enter_context(
                nc.sbuf_tensor(f"ql{nm}", [128, 4 * 512], BF16))

            qstk = ExitStack()
            tbufs = [qstk.enter_context(
                nc.sbuf_tensor(f"qt{i}{nm}", [128, 256], F32))
                for i in range(4)]
            tbufs = tbufs + [tbufs[3]]
            kt1 = qstk.enter_context(
                nc.sbuf_tensor(f"qk0{nm}", [128, 256], F32))
            ktb = [kt1, tbufs[2], tbufs[3]]
            xstg2 = qstk.enter_context(
                nc.sbuf_tensor(f"qxs{nm}", [128, 256], F32))
            xtd_hi = qstk.enter_context(
                nc.sbuf_tensor(f"qxh{nm}", [128, 2 * 256], BF16))
            xtd_lo = qstk.enter_context(
                nc.sbuf_tensor(f"qxl{nm}", [128, 2 * 256], BF16))
            wf = qstk.enter_context(
                nc.sbuf_tensor(f"qwf{nm}", [128, 512], F32))
            wh = qstk.enter_context(
                nc.sbuf_tensor(f"qwh{nm}", [128, 1024], BF16))
            wl = qstk.enter_context(
                nc.sbuf_tensor(f"qwl{nm}", [128, 1024], BF16))
            TRQ = (4, 5)
            for half in range(2 - (1 if STRIP >= 2 else 0)):
                tok0 = tile * 512 + half * 256
                if STRIP >= 2:
                    mcos = msin = None
                else:
                    mcos, msin = self.gen_tables(tbufs, self.pos_own, tok0,
                                                 256)
                qstart = [False] * 4
                for d in range(16):
                    s_ = d % 2
                    xsl = slice(s_ * 256, (s_ + 1) * 256)
                    tb = TRQ[d % 2]
                    xms = []
                    for b2 in range(2):
                        xms.append(self.dma(
                            "gpsimd", self.SDG,
                            xstg2[:, b2 * 128:(b2 + 1) * 128],
                            self.x_own[tok0 + b2 * 128:tok0 + b2 * 128 + 128,
                                       d * 128:(d + 1) * 128],
                            waits=list(self.wslot.get(("qx",), ()))))
                    tmk = None
                    for b2 in range(2):
                        tmk = self.op(
                            "tensor",
                            lambda e, b2=b2, tb=tb, xstg2=xstg2: e.transpose(
                                self.pb[tb][:, b2 * 128:(b2 + 1) * 128],
                                xstg2[:, b2 * 128:(b2 + 1) * 128],
                                self.identF_s[:, :]),
                            waits=(xms + [self.cm_id] + self.bw(tb))
                            if b2 == 0 else (),
                            inc=self.SP if b2 == 1 else None)
                    self.bank_done[tb] = tmk
                    self.wslot[("qx",)] = [tmk]
                    hmx = self.op("scalar",
                                  lambda e, xsl=xsl, tb=tb, xtd_hi=xtd_hi:
                                      e.copy(
                                          xtd_hi[:, xsl],
                                          self.pb[tb][:, :256]),
                                  waits=[tmk]
                                  + list(self.wslot.get(("qxt", s_), ())),
                                  inc=SS)
                    lmx = self.op("vector",
                                  lambda e, xsl=xsl, tb=tb, xtd_hi=xtd_hi,
                                  xtd_lo=xtd_lo:
                                      e.scalar_tensor_tensor(
                                          xtd_lo[:, xsl],
                                          self.pb[tb][:, :256], 1.0,
                                          xtd_hi[:, xsl],
                                          ALU.mult, ALU.subtract),
                                  waits=[hmx], inc=SV)
                    self.bank_free[tb] = lmx
                    wsl_ = slice(s_ * 512, (s_ + 1) * 512)
                    dm = self.dma("sync", self.SDW, wf[:, :],
                                  self.wq_w[d * 128:(d + 1) * 128,
                                            kvg * 512:(kvg + 1) * 512],
                                  waits=self.wslot.get(("qw", "f"), ()))
                    hm = self.op("scalar",
                                 lambda e, wsl_=wsl_, wh=wh, wf=wf: e.copy(
                                     wh[:, wsl_], wf[:, :]),
                                 waits=[dm] + list(self.wslot.get(
                                     ("qw", "hl", s_), ())), inc=SS)
                    lm = self.op("vector",
                                 lambda e, wsl_=wsl_, wl=wl, wf=wf, wh=wh:
                                     e.scalar_tensor_tensor(
                                         wl[:, wsl_], wf[:, :], 1.0,
                                         wh[:, wsl_], ALU.mult, ALU.subtract),
                                 waits=[hm], inc=SV)
                    self.wslot[("qw", "f")] = [lm]
                    if tile == 0 and kvg == 0 and half == 0 and d == 15:
                        self.tap("wf15", wf[:, :], [128, 512], F32)
                        self.tap("wh15", wh[:, :], [128, 1024], BF16)
                        self.tap("wl15", wl[:, :], [128, 1024], BF16)
                        self.tap("xtd15", xtd_hi[:, :], [128, 512], BF16)
                        self.tap("xtdl15", xtd_lo[:, :], [128, 512], BF16)
                    for hh in range(4):
                        bi = hh
                        lh = wh[:, s_ * 512 + hh * 128:
                                s_ * 512 + hh * 128 + 128]
                        ll = wl[:, s_ * 512 + hh * 128:
                                s_ * 512 + hh * 128 + 128]
                        rh = xtd_hi[:, xsl]
                        rl = xtd_lo[:, xsl]
                        pairs = [(lh, rh), (lh, rl), (ll, rh)]
                        for i, (l, r) in enumerate(pairs):
                            first = not qstart[hh] and i == 0
                            last = d == 15 and i == 2
                            w = ([hm, lm, hmx, lmx] + self.bw(bi)) if first \
                                else ([hm, lm, hmx, lmx] if i == 0 else ())
                            mk = self.op(
                                "tensor",
                                lambda e, l=l, r=r, a=first, o=last, bi=bi:
                                    e.matmul(self.pb[bi][:, :256], l, r,
                                             start=a, stop=o),
                                waits=w, inc=SP if last else None)
                            if last:
                                self.bank_done[bi] = mk
                        qstart[hh] = True
                    tm = self.tick()
                    self.wslot[("qw", "hl", s_)] = [tm]
                    self.wslot[("qxt", s_)] = [tm]
                for hh in range(4):
                    h_abs = kvg * 4 + hh
                    mlr = self.rope_split(
                        hh, 256, self.wqb_s[:, h_abs:h_abs + 1],
                        QT_hi[:, hh * 512 + half * 256:
                              hh * 512 + half * 256 + 256],
                        QT_lo[:, hh * 512 + half * 256:
                              hh * 512 + half * 256 + 256],
                        ktb, (mcos, msin, tbufs[0], tbufs[1]),
                        tapname=("q00" if (tile == 0 and kvg == 0
                                           and half == 0 and hh == 0)
                                 else None))
                self.wslot[("posb", id(tbufs[4]))] = [mlr]
            self.close_scope(qstk)
            q_ready = [(SV, SV.n), (SS, SS.n)]
            if tile == 0 and kvg == 0:
                self.tap("qthi", QT_hi[:, :], [128, 2048], BF16)
            if tile == 1 and kvg == 0:
                self.tap("qthi_t1", QT_hi[:, :], [128, 2048], BF16)
                self.tap("qtlo_t1", QT_lo[:, :], [128, 2048], BF16)
                self.tap("qtlo", QT_lo[:, :], [128, 2048], BF16)

            if STRIP:
                self.close_scope(qascope)
                self.close_scope(kstk)
                self.close_scope(tstk)
                return
            astk = ExitStack()
            scores = astk.enter_context(
                nc.sbuf_tensor(f"sc{nm}", [128, 4096], F32))
            pfh = astk.enter_context(
                nc.sbuf_tensor(f"pfh{nm}", [128, 2 * 512], BF16))
            pfl = astk.enter_context(
                nc.sbuf_tensor(f"pfl{nm}", [128, 2 * 512], BF16))
            pth = astk.enter_context(
                nc.sbuf_tensor(f"pth{nm}", [128, 2 * 512], BF16))
            ptl = astk.enter_context(
                nc.sbuf_tensor(f"ptl{nm}", [128, 2 * 512], BF16))
            aof = astk.enter_context(
                nc.sbuf_tensor(f"aof{nm}", [128, 128], F32))

            for bl in range(4):
                bi_own = tile * 4 + bl
                vis = (2 * bi_own + 2) * 128
                nkt = (vis + 511) // 512
                for hh in range(4):
                    self.attn_head(tile, kvg, bl, hh, vis, nkt, QT_hi, QT_lo,
                                   scores, pfh, pfl, pth, ptl, aof,
                                   ao_hi, ao_lo, q_ready)
            if tile == 0 and kvg == 0:
                self.tap("aohi", ao_hi[:, :], [128, 2048], BF16)
                self.tap("aolo", ao_lo[:, :], [128, 2048], BF16)
            self.close_scope(astk)
            self.close_scope(qascope)
            ao_ready = [(SV, SV.n), (SS, SS.n)]

            self.o_proj(tile, kvg, acc, acc_mark, ao_hi, ao_lo, ao_ready)
            self.close_scope(kstk)

        if tile == 0:
            self.tap("acc_pre", acc[:, :], [128, 8192], F32)
        if tile == 1:
            self.tap("acc_pre_t1", acc[:, :], [128, 8192], F32)
        self.emit_tail(tile, acc, acc_mark, tstk)
        self.close_scope(tstk)

    def attn_head(self, tile, kvg, bl, hh, vis, nkt, QT_hi, QT_lo, scores,
                  pfh, pfl, pth, ptl, aof, ao_hi, ao_lo, q_ready):
        SP, SV, SS = self.SP, self.SV, self.SS
        if not hasattr(self, "_pf_gate"):
            self._pf_gate = {}
        pfg = self._pf_gate
        mx_marks = []
        for kt in range(nkt):
            wkt = min(512, vis - kt * 512)
            sbnk = kt % 2
            lh = QT_hi[:, hh * 512 + bl * 128:hh * 512 + bl * 128 + 128]
            ll = QT_lo[:, hh * 512 + bl * 128:hh * 512 + bl * 128 + 128]
            rh = self.K_ap(0, kt, kvg, 0, wkt)
            rl = self.K_ap(1, kt, kvg, 0, wkt)
            self.mm_group(sbnk, [(lh, rh), (lh, rl), (ll, rh)], wkt,
                          waits=list(q_ready))
            if kt == nkt - 1:
                moff = wkt - 256
                mmk = self.op("vector",
                              lambda e, sbnk=sbnk, moff=moff: e.tensor_add(
                                  self.pb[sbnk][:, moff:moff + 256],
                                  self.pb[sbnk][:, moff:moff + 256],
                                  self.maskAB_s[:, :]),
                              waits=[self.bank_done[sbnk], self.cm_msk],
                              inc=SV)
                self.bank_done[sbnk] = mmk
            self.evac("scalar", sbnk,
                      lambda e, kt=kt, wkt=wkt, sbnk=sbnk:
                          e.copy(scores[:, kt * 512:kt * 512 + wkt],
                                 self.pb[sbnk][:, :wkt]),
                      waits=[pfg.get("scores")] if pfg.get("scores") else ())
            mx_marks.append(self.op(
                "vector",
                lambda e, kt=kt, wkt=wkt: e.reduce_max(
                    self.maxbuf[:, kt:kt + 1],
                    scores[:, kt * 512:kt * 512 + wkt], AX.X),
                waits=[(SS, SS.n)], inc=SV))
        if tile == 0 and kvg == 0 and bl == 0 and hh == 0:
            self.tap("sc_raw", scores[:, :min(vis, 512)],
                     [128, min(vis, 512)], F32)
        mneg = self.op("vector", lambda e, nkt=nkt: e.tensor_reduce(
            self.stats[:, 0:1], self.maxbuf[:, :nkt], AX.X, ALU.max,
            negate=True),
            waits=[self.last_exp] if self.last_exp else (), inc=SV)
        mb = self.op("vector", lambda e: e.tensor_single_scalar(
            self.stats[:, 1:2], self.stats[:, 0:1], SCALE, ALU.mult),
            inc=SV)
        mexp = self.op("scalar", lambda e, vis=vis: e.activation(
            scores[:, :vis], scores[:, :vis], AF.Exp,
            bias=self.stats[:, 1:2], scale=SCALE,
            accum_out=self.rowsum[:, 0:1]), waits=[mb], inc=SS)
        self.last_exp = mexp
        if tile == 0 and kvg == 0 and bl == 0 and hh == 0:
            self.tap("sc_exp", scores[:, :min(vis, 512)],
                     [128, min(vis, 512)], F32)
        mr = self.op("vector", lambda e: e.reciprocal(
            self.rinv[:, 0:1], self.rowsum[:, 0:1]), waits=[mexp], inc=SV)
        mr = self.op("scalar", lambda e: e.copy(
            self.stats[:, 2:3], self.rinv[:, 0:1]), waits=[mr], inc=SS)
        av_started = False
        avm = None
        for kt in range(nkt):
            wkt = min(512, vis - kt * 512)
            s2 = kt % 2
            mhi = self.op("vector",
                          lambda e, kt=kt, wkt=wkt, s2=s2: e.tensor_scalar(
                              pfh[:, s2 * 512:s2 * 512 + wkt],
                              scores[:, kt * 512:kt * 512 + wkt],
                              self.rinv[:, 0:1], None, ALU.mult),
                          waits=[mr] + ([pfg.get(("pf", s2))]
                                        if pfg.get(("pf", s2)) else []),
                          inc=SV)
            mlo = self.op("vector",
                          lambda e, kt=kt, wkt=wkt, s2=s2:
                              e.scalar_tensor_tensor(
                                  pfl[:, s2 * 512:s2 * 512 + wkt],
                                  scores[:, kt * 512:kt * 512 + wkt],
                                  self.rinv[:, 0:1],
                                  pfh[:, s2 * 512:s2 * 512 + wkt],
                                  ALU.mult, ALU.subtract),
                          waits=[mhi], inc=SV)
            nch = wkt // 128
            tb = 6 + (kt % 2)
            ptb = self.pt[kt % 2]
            w0 = [mhi, mlo, self.cm_idb] + self.bw(tb)
            tmk = None
            for i in range(nch):
                tmk = self.op(
                    "tensor",
                    lambda e, i=i, s2=s2, ptb=ptb: e.transpose(
                        ptb[:, i * 128:(i + 1) * 128],
                        pfh[:, s2 * 512 + i * 128:s2 * 512 + (i + 1) * 128],
                        self.identB_s[:, :]),
                    waits=w0 if i == 0 else (), inc=None)
            for i in range(nch):
                tmk = self.op(
                    "tensor",
                    lambda e, i=i, s2=s2, ptb=ptb, nch=nch: e.transpose(
                        ptb[:, (nch + i) * 128:(nch + i + 1) * 128],
                        pfl[:, s2 * 512 + i * 128:s2 * 512 + (i + 1) * 128],
                        self.identB_s[:, :]),
                    waits=(), inc=SP if i == nch - 1 else None)
            self.bank_done[tb] = tmk
            pfg[("pf", s2)] = (SP, SP.n)
            ehm = self.op("vector",
                          lambda e, s2=s2, nch=nch, ptb=ptb: e.tensor_copy(
                              pth[:, s2 * 512:s2 * 512 + nch * 128],
                              ptb[:, :nch * 128]),
                          waits=[tmk] + ([pfg.get(("pt", s2))]
                                         if pfg.get(("pt", s2)) else []),
                          inc=SV)
            elm = self.op("vector",
                          lambda e, s2=s2, nch=nch, ptb=ptb: e.tensor_copy(
                              ptl[:, s2 * 512:s2 * 512 + nch * 128],
                              ptb[:, nch * 128:2 * nch * 128]),
                          waits=[tmk], inc=SV)
            self.bank_free[tb] = elm
            for c in range(nch):
                vh = self.V_ap(0, kt, c, kvg * 128, 128)
                vl = self.V_ap(1, kt, c, kvg * 128, 128)
                prh = pth[:, s2 * 512 + c * 128:s2 * 512 + (c + 1) * 128]
                prl = ptl[:, s2 * 512 + c * 128:s2 * 512 + (c + 1) * 128]
                pairs = [(vh, prh), (vh, prl), (vl, prh)]
                for i, (l, r) in enumerate(pairs):
                    first = (not av_started) and i == 0
                    last = (kt == nkt - 1 and c == nch - 1 and i == 2)
                    w = ([ehm, elm] + self.bw(5)) if first else \
                        ([ehm, elm] if i == 0 else ())
                    avm = self.op(
                        "tensor",
                        lambda e, l=l, r=r, a=first, o=last:
                            e.matmul(self.pb[5][:, :128], l, r,
                                     start=a, stop=o),
                        waits=w, inc=SP if last else None)
                av_started = True
            if kt < nkt - 1:
                pfg[("pt", s2)] = self.tick()
            else:
                pfg[("pt", s2)] = avm
        self.bank_done[5] = avm
        mev = self.evac("vector", 5,
                        lambda e: e.tensor_copy(aof[:, :],
                                                self.pb[5][:, :128]))
        col = (hh * 4 + bl) * 128
        mh2 = self.op("scalar", lambda e, col=col: e.copy(
            ao_hi[:, col:col + 128], aof[:, :]), waits=[mev], inc=SS)
        self.wslot[("aof", id(aof))] = [mh2]
        self.op("vector", lambda e, col=col: e.scalar_tensor_tensor(
            ao_lo[:, col:col + 128], aof[:, :], 1.0,
            ao_hi[:, col:col + 128], ALU.mult, ALU.subtract),
            waits=[mh2], inc=SV)
        pfg["scores"] = (SV, self.SV.n)

    def o_proj(self, tile, kvg, acc, acc_mark, ao_hi, ao_lo, ao_ready):
        nc = self.nc
        SP, SV, SS, SG = self.SP, self.SV, self.SS, self.SG
        ostk = ExitStack()
        nm = f"{tile}_{kvg}"
        wstg = ostk.enter_context(nc.sbuf_tensor(f"owf{nm}", [128, 2048], F32))
        woh = ostk.enter_context(
            nc.sbuf_tensor(f"owh{nm}", [128, 2048], BF16))
        wol = ostk.enter_context(
            nc.sbuf_tensor(f"owl{nm}", [128, 2048], BF16))
        xres = ostk.enter_context(nc.sbuf_tensor(f"oxr{nm}", [128, 2048], F32))
        obrow = ostk.enter_context(nc.sbuf_tensor(f"obr{nm}", [128, 512], F32))
        for of in range(4):
            dmw = self.dma("sync", self.SDW,
                           wstg[:, :].rearrange("p (c f) -> p c f", c=4),
                           self.o_w[kvg * 512:(kvg + 1) * 512,
                                    of * 512:(of + 1) * 512]
                           .rearrange("(c p) f -> p c f", p=128),
                           waits=self.wslot.get(("wo", "f"), ()))
            hm, lm = self.split(wstg[:, :], woh[:, :], wol[:, :],
                                waits=[dmw]
                                + list(self.wslot.get(("wo", "hl"), ())))
            self.wslot[("wo", "f")] = [lm]
            for bl in range(4):
                if kvg == 0:
                    dmx = self.dma("gpsimd", self.SDG,
                                   xres[:, of * 512:(of + 1) * 512],
                                   self.x_own[(tile * 4 + bl) * 128:
                                              (tile * 4 + bl) * 128 + 128,
                                              of * 512:(of + 1) * 512],
                                   waits=[acc_mark[0]] if acc_mark[0] else ())
                if True:
                    ybank = 2 + (bl % 2)
                    if kvg == 0:
                        dmb = self.dma("gpsimd", self.SDG, obrow[0:1, :],
                                       self.o_b
                                       .rearrange("(a f) -> a f", a=1)
                                       [0:1, of * 512:(of + 1) * 512],
                                       waits=list(self.wslot.get(
                                           ("ob",), ())))
                    pairs = []
                    for hh in range(4):
                        colA = (hh * 4 + bl) * 128
                        pairs += [
                            (ao_hi[:, colA:colA + 128],
                             woh[:, hh * 512:(hh + 1) * 512]),
                            (ao_hi[:, colA:colA + 128],
                             wol[:, hh * 512:(hh + 1) * 512]),
                            (ao_lo[:, colA:colA + 128],
                             woh[:, hh * 512:(hh + 1) * 512])]
                    w = [hm, lm] + list(ao_ready)
                    if kvg == 0:
                        pairs = [(self.ones1[0:1, :], obrow[0:1, :])] + pairs
                        w += [dmb, self.m_ones]
                    self.mm_group(ybank, pairs, 512, waits=w)
                    if kvg == 0:
                        self.wslot[("ob",)] = [(SP, SP.n)]
                    accsl = acc[:, bl * 2048 + of * 512:
                                bl * 2048 + (of + 1) * 512]
                    if kvg == 0:
                        am = self.evac(
                            "vector", ybank,
                            lambda e, ybank=ybank, accsl=accsl, of=of:
                                e.tensor_add(accsl, self.pb[ybank][:, :512],
                                             xres[:, of * 512:
                                                  (of + 1) * 512]),
                            waits=[dmx])
                    else:
                        am = self.evac(
                            "vector", ybank,
                            lambda e, ybank=ybank, accsl=accsl:
                                e.tensor_add(accsl, self.pb[ybank][:, :512],
                                             accsl))
                    acc_mark[0] = am
            self.wslot[("wo", "hl")] = [(SP, SP.n)]
        self.close_scope(ostk)

    def layernorm(self, acc, tc, scratch, acc_mark):
        SV, SS = self.SV, self.SS
        sl = acc[:, tc * 2048:(tc + 1) * 2048]
        stats = self.stats
        mneg = self.op("vector", lambda e: e.tensor_reduce(
            stats[:, 4:5], sl, AX.X, ALU.add, negate=True),
            waits=[acc_mark[0]] if acc_mark[0] else (), inc=SV)
        mc5 = self.op("scalar", lambda e: e.activation(
            stats[:, 5:6], stats[:, 4:5], AF.Identity, scale=1.0 / 2048.0),
            waits=[mneg], inc=SS)
        mc = self.op("vector", lambda e: e.tensor_scalar(
            sl, sl, stats[:, 5:6], None, ALU.add), waits=[mc5], inc=SV)
        msq = self.op("scalar", lambda e: e.activation(
            scratch[:, :1024], sl[:, 0:1024], AF.Square,
            accum_out=stats[:, 14:15]), waits=[mc], inc=SS)
        msq = self.op("scalar", lambda e: e.activation(
            scratch[:, :1024], sl[:, 1024:2048], AF.Square,
            accum_out=stats[:, 15:16]), inc=SS)
        mssq = self.op("vector", lambda e: e.tensor_add(
            stats[:, 7:8], stats[:, 14:15], stats[:, 15:16]),
            waits=[msq], inc=SV)
        msd = self.op("scalar", lambda e: e.activation(
            stats[:, 8:9], stats[:, 7:8], AF.Sqrt, bias=self.epsb[:, 0:1],
            scale=1.0 / 2048.0), waits=[mssq, self.m_eps], inc=SS)
        mvp = self.op("scalar", lambda e: e.activation(
            stats[:, 9:10], stats[:, 7:8], AF.Identity,
            bias=self.epsb[:, 0:1], scale=1.0 / 2048.0), inc=SS)
        mr0 = self.op("vector", lambda e: e.reciprocal(
            stats[:, 10:11], stats[:, 8:9]), waits=[msd, mvp], inc=SV)
        ms2 = self.op("scalar", lambda e: e.activation(
            stats[:, 11:12], stats[:, 10:11], AF.Square), waits=[mr0],
            inc=SS)
        mb2 = self.op("vector", lambda e: e.tensor_mul(
            stats[:, 11:12], stats[:, 11:12], stats[:, 9:10]),
            waits=[ms2], inc=SV)
        mfin = self.op("scalar", lambda e: e.activation(
            stats[:, 11:12], stats[:, 11:12], AF.Identity,
            bias=self.b15[:, 0:1], scale=-0.5),
            waits=[mb2, self.m_b15], inc=SS)
        mm2 = self.op("vector", lambda e: e.tensor_mul(
            stats[:, 12:13], stats[:, 10:11], stats[:, 11:12]),
            waits=[mfin], inc=SV)
        mdr = self.op("scalar", lambda e: e.copy(
            stats[:, 13:14], stats[:, 12:13]), waits=[mm2], inc=SS)
        mf = self.op("vector", lambda e: e.tensor_scalar(
            sl, sl, stats[:, 12:13], None, ALU.mult), waits=[mdr], inc=SV)
        acc_mark[0] = mf
        return mf

    def emit_tail(self, tile, acc, acc_mark, tstk):
        nc = self.nc
        SP, SV, SS, SG = self.SP, self.SV, self.SS, self.SG
        fstk = ExitStack()
        nm = f"t{tile}"
        wstg = fstk.enter_context(nc.sbuf_tensor(f"fwf{nm}", [128, 1024], F32))
        wsh = [fstk.enter_context(
            nc.sbuf_tensor(f"fwh{i}{nm}", [128, 2048], BF16))
            for i in range(3)]
        wsl = [fstk.enter_context(
            nc.sbuf_tensor(f"fwl{i}{nm}", [128, 2048], BF16))
            for i in range(3)]
        hbh = fstk.enter_context(
            nc.sbuf_tensor(f"fhh{nm}", [128, 3 * 512], BF16))
        hbl = fstk.enter_context(
            nc.sbuf_tensor(f"fhl{nm}", [128, 3 * 512], BF16))
        sbu = fstk.enter_context(nc.sbuf_tensor(f"fsb{nm}", [128, 512], F32))
        tmpf = fstk.enter_context(nc.sbuf_tensor(f"ftm{nm}", [128, 512], F32))
        brow = fstk.enter_context(nc.sbuf_tensor(f"fbr{nm}", [128, 512], F32))

        for tc in range(4):
            self.layernorm(acc, tc, wstg, acc_mark)
            if tile == 0 and tc == 0:
                self.tap("ln_stats", self.stats[:, 4:16], [128, 12], F32)
        self.wslot[("fw", "f")] = [(SS, SS.n)]
        ln1_done = [(SV, SV.n)]
        if tile == 0:
            self.tap("acc_ln1", acc[:, :], [128, 8192], F32)
        if tile == 1:
            self.tap("acc_ln1_t1", acc[:, :], [128, 8192], F32)

        TRB = (3, 4)
        ym = None
        for tc in range(4):
            for g0 in range(0, 16, 4):
                bi = TRB[(g0 // 4) % 2]
                w = list(ln1_done) + [self.cm_id] + self.bw(bi)
                mk = None
                for i in range(4):
                    c = g0 + i
                    mk = self.op(
                        "tensor",
                        lambda e, c=c, i=i, bi=bi, tc=tc: e.transpose(
                            self.pb[bi][:, i * 128:(i + 1) * 128],
                            acc[:, tc * 2048 + c * 128:
                                tc * 2048 + (c + 1) * 128],
                            self.identF_s[:, :]),
                        waits=w if i == 0 else (),
                        inc=SP if i == 3 else None)
                self.bank_done[bi] = mk

                def ydst(limb, g0=g0, tc=tc, tile=tile):
                    u = (2 * tile + 2 + limb) % 8
                    base = u * 8192
                    a = self.kvblob[:, base:base + 8192].rearrange(
                        "p (c r) -> p c r", c=16)
                    return a[:, g0:g0 + 4, tc * 128:tc * 128 + 128]

                hm = self.op("scalar",
                             lambda e, g0=g0, bi=bi, tc=tc:
                                 e.copy(ydst(0, g0, tc),
                                        self.pb[bi][:, :512]),
                             waits=[mk], inc=SS)
                ym = self.op("vector",
                             lambda e, g0=g0, bi=bi, tc=tc:
                                 e.scalar_tensor_tensor(
                                     ydst(1, g0, tc),
                                     self.pb[bi][:, :512], 1.0,
                                     ydst(0, g0, tc),
                                     ALU.mult, ALU.subtract),
                             waits=[mk, hm], inc=SV)
                self.bank_free[bi] = ym
        y_ready = [(SV, SV.n), (SS, SS.n)]

        if tile == 0:
            self.tap("yt", self.kvblob[:, 2 * 8192:4 * 8192], [128, 16384],
                     BF16)

        def yT_h(c, tok0, n):
            return self.yT_ap(0, tile, c, tok0, n)

        def yT_l(c, tok0, n):
            return self.yT_ap(1, tile, c, tok0, n)

        for tc in range(4):
            pairs = []
            for d in range(16):
                lh = yT_h(d, tc * 128, 128)
                ll = yT_l(d, tc * 128, 128)
                rh = self.rwT_hi[:, d * 8:(d + 1) * 8]
                rl = self.rwT_lo[:, d * 8:(d + 1) * 8]
                pairs += [(lh, rh), (lh, rl), (ll, rh)]
            self.mm_group(5, pairs, 8, waits=list(y_ready) + [self.m_rw])
            mle = self.evac("scalar", 5, lambda e: e.copy(
                self._logit[:, :], self.pb[5][:, :8]))
            self.router_combine(tc, mle)

        if tile == 0:
            self.tap("ctile", self.c_tile[:, :], [128, 32], F32)
        if tile == 1:
            self.tap("ctile_t1", self.c_tile[:, :], [128, 32], F32)
        for e_i in range(NEXP):
            self.expert(tile, e_i, wstg, wsh, wsl, hbh, hbl, sbu, tmpf,
                        brow, acc, acc_mark, y_ready, yT_h, yT_l)
        self.shared_expert(tile, wstg, wsh, wsl, hbh, hbl, sbu, tmpf, brow,
                           acc, acc_mark, y_ready, yT_h, yT_l)

        if tile == 0:
            self.tap("acc_ffn", acc[:, :], [128, 8192], F32)
        if tile == 1:
            self.tap("acc_ffn_t1", acc[:, :], [128, 8192], F32)
        for tc in range(4):
            self.layernorm(acc, tc, wstg, acc_mark)
            self.dma_out(
                self.z[(tile * 4 + tc) * 128:(tile * 4 + tc) * 128 + 128, :],
                acc[:, tc * 2048:(tc + 1) * 2048],
                waits=[acc_mark[0]])
        self.close_scope(fstk)

    def router_combine(self, tc, mle):
        SV, SS = self.SV, self.SS
        lg = self._logit
        l2 = self._l2
        sel = self._sel
        eb = self._eb
        st_ = self.stats
        m1 = self.op("vector", lambda e: e.tensor_reduce(
            st_[:, 0:1], lg[:, :], AX.X, ALU.max, negate=True),
            waits=[mle] + list(self.wslot.get(("rtst",), ())), inc=SV)
        m1d = self.op("scalar", lambda e: e.copy(
            st_[:, 1:2], st_[:, 0:1]), waits=[m1], inc=SS)
        mq = self.op("vector", lambda e: e.tensor_scalar(
            l2[:, :], lg[:, :], st_[:, 0:1], None, ALU.add), waits=[m1d],
            inc=SV)
        mqr = self.op("scalar", lambda e: e.copy(
            st_[:, 3:4], l2[:, 0:1]), waits=[mq], inc=SS)
        mq = self.op("vector", lambda e: e.tensor_single_scalar(
            sel[:, :], l2[:, :], 0.0, ALU.is_ge), waits=[mqr],
            inc=SV)
        mqr = self.op("scalar", lambda e: e.copy(
            st_[:, 3:4], sel[:, 0:1]), waits=[mq], inc=SS)
        mq = self.op("vector", lambda e: e.scalar_tensor_tensor(
            l2[:, :], sel[:, :], NEG, lg[:, :], ALU.mult, ALU.add),
            waits=[mqr], inc=SV)
        mqd = self.op("scalar", lambda e: e.copy(
            st_[:, 3:4], l2[:, 0:1]), waits=[mq], inc=SS)
        m2 = self.op("vector", lambda e: e.tensor_reduce(
            st_[:, 2:3], l2[:, :], AX.X, ALU.max), waits=[mqd], inc=SV)
        me = self.op("scalar", lambda e: e.activation(
            eb[:, :], lg[:, :], AF.Exp, bias=st_[:, 0:1]), waits=[m2],
            inc=SS)
        me2 = self.op("scalar", lambda e: e.activation(
            st_[:, 4:5], st_[:, 2:3], AF.Exp, bias=st_[:, 0:1]), inc=SS)
        self.wslot[("rtst",)] = [me2]
        msel = self.op("vector", lambda e: e.tensor_scalar(
            sel[:, :], lg[:, :], st_[:, 2:3], None, ALU.is_ge),
            waits=[me2], inc=SV)
        mden = self.op("vector", lambda e: e.tensor_scalar_add(
            st_[:, 5:6], st_[:, 4:5], 1.0), waits=[me2], inc=SV)
        mdb = self.op("scalar", lambda e: e.copy(
            st_[:, 8:9], st_[:, 5:6]), waits=[mden], inc=SS)
        mrc = self.op("vector", lambda e: e.reciprocal(
            st_[:, 6:7], st_[:, 8:9]), waits=[mdb], inc=SV)
        mdr = self.op("scalar", lambda e: e.copy(
            st_[:, 7:8], st_[:, 6:7]), waits=[mrc], inc=SS)
        mselr = self.op("scalar", lambda e: e.copy(
            st_[:, 9:10], sel[:, 0:1]), waits=[msel], inc=SS)
        self.op("vector", lambda e, tc=tc: e.scalar_tensor_tensor(
            self.c_tile[:, tc * 8:(tc + 1) * 8], eb[:, :], st_[:, 6:7],
            sel[:, :], ALU.mult, ALU.mult), waits=[mdr, me, mselr], inc=SV)

    def halfstream(self, wstg, dsth, dstl, src_ap, slotkey, plain=False):
        hms, lms = [], []
        for h2 in range(2):
            if plain:
                src = src_ap[:, h2 * 1024:(h2 + 1) * 1024]
                dst = wstg[:, :]
            else:
                src = src_ap[:, 8 * h2:8 * h2 + 8, :]
                dst = wstg[:, :].rearrange("p (c f) -> p c f", c=8)
            dm = self.dma("sync", self.SDW, dst, src,
                          waits=self.wslot.get(("fw", "f"), ()))
            hm, lm = self.split(wstg[:, :],
                                dsth[:, h2 * 1024:(h2 + 1) * 1024],
                                dstl[:, h2 * 1024:(h2 + 1) * 1024],
                                waits=[dm] + (list(self.wslot.get(
                                    ("fw", "hl", slotkey), ()))
                                    if h2 == 0 else []))
            self.wslot[("fw", "f")] = [lm]
            hms.append(hm)
            lms.append(lm)
        return hms[-1], lms[-1]

    def gateup_chunk(self, m, slot, wsrc_g, wsrc_u, wstg, wsh, wsl, hbh, hbl,
                     sbu, tmpf, yT_h, yT_l, y_ready, bias_g=None, bias_u=None):
        SP, SV, SS, SG = self.SP, self.SV, self.SS, self.SG
        ws = slot % 3
        hmg, lmg = self.halfstream(wstg, wsh[ws], wsl[ws], wsrc_g, ws)
        gb = 0
        pairs = []
        for d in range(16):
            pairs += [(wsh[ws][:, d * 128:(d + 1) * 128], yT_h(d, 0, 512)),
                      (wsh[ws][:, d * 128:(d + 1) * 128], yT_l(d, 0, 512)),
                      (wsl[ws][:, d * 128:(d + 1) * 128], yT_h(d, 0, 512))]
        self.mm_group(gb, pairs, 512,
                      waits=[hmg, lmg] + list(y_ready))
        self.wslot[("fw", "hl", ws)] = [(SP, SP.n)]
        sg_prev = list(self.wslot.get(("sbu", id(sbu)), ()))
        if bias_g is None:
            msil = self.evac("scalar", gb, lambda e: e.activation(
                sbu[:, :], self.pb[gb][:, :512], AF.Silu), waits=sg_prev)
        else:
            msil = self.evac("scalar", gb, lambda e, bias_g=bias_g:
                             e.activation(sbu[:, :], self.pb[gb][:, :512],
                                          AF.Silu, bias=bias_g),
                             waits=[self.cm_sgb] + sg_prev)
        ws2 = (slot + 1) % 3
        hmu, lmu = self.halfstream(wstg, wsh[ws2], wsl[ws2], wsrc_u, ws2)
        ub = 1
        pairs = []
        for d in range(16):
            pairs += [(wsh[ws2][:, d * 128:(d + 1) * 128], yT_h(d, 0, 512)),
                      (wsh[ws2][:, d * 128:(d + 1) * 128], yT_l(d, 0, 512)),
                      (wsl[ws2][:, d * 128:(d + 1) * 128], yT_h(d, 0, 512))]
        self.mm_group(ub, pairs, 512, waits=[hmu, lmu] + list(y_ready))
        self.wslot[("fw", "hl", ws2)] = [(SP, SP.n)]
        hsl = slice((m % 3) * 512, (m % 3) * 512 + 512)
        badd = 0.0 if bias_u is None else bias_u
        mhh = self.op("vector", lambda e, hsl=hsl, badd=badd:
                      e.scalar_tensor_tensor(hbh[:, hsl],
                                             self.pb[ub][:, :512], badd,
                                             sbu[:, :], ALU.add, ALU.mult),
                      waits=[self.bank_done[ub], msil,
                             self.cm_sub] + list(self.wslot.get(
                                 ("h", m % 3), ())), inc=SV)
        mtf = self.op("vector", lambda e, badd=badd: e.scalar_tensor_tensor(
            tmpf[:, :], self.pb[ub][:, :512], badd, sbu[:, :],
            ALU.add, ALU.mult), inc=SV)
        self.bank_free[ub] = mtf
        mhl = self.op("vector", lambda e, hsl=hsl: e.scalar_tensor_tensor(
            hbl[:, hsl], tmpf[:, :], 1.0, hbh[:, hsl], ALU.mult,
            ALU.subtract), waits=[mtf], inc=SV)
        self.wslot[("sbu", id(sbu))] = [mhl]
        return mhh, mhl

    def down_round(self, msrcs, hslots, wsrc_fn, wstg, wsh, wsl, hbh, hbl,
                   acc, acc_mark, ce_ap_fn, brow, bias_src, first_round):
        SP, SV, SS, SG = self.SP, self.SV, self.SS, self.SG
        marks = {}
        for idx, m in enumerate(hslots):
            ws = idx % 3
            hm, lm = self.halfstream(wstg, wsh[ws], wsl[ws], wsrc_fn(m), ws,
                                     plain=True)
            marks[m] = (ws, hm, lm)
        if bias_src is not None and first_round:
            dmb = {}
            for of in range(4):
                dmb[of] = self.dma("gpsimd", self.SDG,
                                   brow[0:1, of * 128:of * 128 + 128]
                                   if False else brow[0:1, :512],
                                   bias_src[0:1, of * 512:(of + 1) * 512],
                                   waits=list(self.wslot.get(("sb", of), ())))
                self.wslot[("sb", of)] = [(SP, SP.n)]
        for tc in range(4):
            for of in range(4):
                dbank = 2 + ((tc * 4 + of) % 2)
                pairs = []
                for m in hslots:
                    ws, hm, lm = marks[m]
                    hsl_h = hbh[:, (m % 3) * 512 + tc * 128:
                                (m % 3) * 512 + tc * 128 + 128]
                    hsl_l = hbl[:, (m % 3) * 512 + tc * 128:
                                (m % 3) * 512 + tc * 128 + 128]
                    rh = wsh[ws][:, of * 512:(of + 1) * 512]
                    rl = wsl[ws][:, of * 512:(of + 1) * 512]
                    pairs += [(hsl_h, rh), (hsl_h, rl), (hsl_l, rh)]
                w = [marks[m][1] for m in hslots] + \
                    [marks[m][2] for m in hslots] + list(msrcs)
                if bias_src is not None and first_round:
                    pass
                self.mm_group(dbank, pairs, 512, waits=w)
                accsl = acc[:, tc * 2048 + of * 512:tc * 2048 + (of + 1) * 512]
                ce = ce_ap_fn(tc)
                if ce is None:
                    am = self.evac("vector", dbank,
                                   lambda e, dbank=dbank, accsl=accsl:
                                       e.tensor_add(accsl,
                                                    self.pb[dbank][:, :512],
                                                    accsl))
                else:
                    am = self.evac("vector", dbank,
                                   lambda e, dbank=dbank, accsl=accsl, ce=ce:
                                       e.scalar_tensor_tensor(
                                           accsl, self.pb[dbank][:, :512],
                                           ce, accsl, ALU.mult, ALU.add))
                acc_mark[0] = am
        for m in hslots:
            self.wslot[("fw", "hl", marks[m][0])] = [(SP, SP.n)]
            self.wslot[("h", m % 3)] = [(SP, SP.n)]

    def expert(self, tile, e_i, wstg, wsh, wsl, hbh, hbl, sbu, tmpf, brow,
               acc, acc_mark, y_ready, yT_h, yT_l):
        slot_ctr = 0
        for r0 in range(0, NIC, 3):
            chunks = list(range(r0, min(r0 + 3, NIC)))
            msrcs = []
            for m in chunks:
                mh, ml = self.gateup_chunk(
                    m, slot_ctr,
                    self.gate_w[e_i][:, m * 128:(m + 1) * 128]
                    .rearrange("(c p) f -> p c f", p=128),
                    self.up_w[e_i][:, m * 128:(m + 1) * 128]
                    .rearrange("(c p) f -> p c f", p=128),
                    wstg, wsh, wsl, hbh, hbl, sbu, tmpf, yT_h, yT_l, y_ready)
                msrcs += [mh, ml]
                slot_ctr += 2
            self.down_round(
                msrcs, chunks,
                lambda m, e_i=e_i: self.down_w[e_i][m * 128:(m + 1) * 128, :],
                wstg, wsh, wsl, hbh, hbl, acc, acc_mark,
                lambda tc: self.c_tile[:, tc * 8 + e_i:tc * 8 + e_i + 1],
                brow, None, False)

    def shared_expert(self, tile, wstg, wsh, wsl, hbh, hbl, sbu, tmpf, brow,
                      acc, acc_mark, y_ready, yT_h, yT_l):
        SP, SG = self.SP, self.SG
        dmb_done = []
        for of in range(4):
            dmb = self.dma("gpsimd", self.SDG, brow[0:1, :],
                           self.s_down_b.rearrange("(a f) -> a f", a=1)
                           [0:1, of * 512:(of + 1) * 512],
                           waits=dmb_done[-1:] if dmb_done else ())
            for tc in range(4):
                bnk = 2 + (tc % 2)
                self.mm_group(bnk, [(self.ones1[0:1, :], brow[0:1, :])], 512,
                              waits=[dmb, self.m_ones])
                accsl = acc[:, tc * 2048 + of * 512:tc * 2048 + (of + 1) * 512]
                am = self.evac("vector", bnk,
                               lambda e, bnk=bnk, accsl=accsl:
                                   e.tensor_add(accsl, self.pb[bnk][:, :512],
                                                accsl))
                acc_mark[0] = am
            dmb_done.append(am)
        slot_ctr = 0
        for r0 in range(0, NSC, 3):
            chunks = list(range(r0, min(r0 + 3, NSC)))
            msrcs = []
            for m in chunks:
                mh, ml = self.gateup_chunk(
                    m, slot_ctr,
                    self.s_gate_w[:, m * 128:(m + 1) * 128]
                    .rearrange("(c p) f -> p c f", p=128),
                    self.s_up_w[:, m * 128:(m + 1) * 128]
                    .rearrange("(c p) f -> p c f", p=128),
                    wstg, wsh, wsl, hbh, hbl, sbu, tmpf, yT_h, yT_l, y_ready,
                    bias_g=self.sgb_s[:, m:m + 1], bias_u=self.sub_s[:, m:m + 1])
                msrcs += [mh, ml]
                slot_ctr += 2
            self.down_round(
                msrcs, chunks,
                lambda m: self.s_down_w[m * 128:(m + 1) * 128, :],
                wstg, wsh, wsl, hbh, hbl, acc, acc_mark,
                lambda tc: None, brow, None, False)


def _host_consts(par):
    tri = np.where(np.triu(np.ones((128, 128), bool), 1), np.float32(NEG),
                   np.float32(0.0)).astype(np.float32)
    full = np.full((128, 128), np.float32(NEG), dtype=np.float32)
    zero = np.zeros((128, 128), dtype=np.float32)
    if par == 0:
        maskAB = np.concatenate([tri, full], axis=1)
    else:
        maskAB = np.concatenate([zero, tri], axis=1)
    identF = np.eye(128, dtype=np.float32)
    identB = np.eye(128, dtype=np.float32).astype(ml_dtypes.bfloat16)
    R = np.zeros((128, 128), dtype=np.float32)
    for j in range(64):
        R[j, j + 64] = -1.0
        R[j + 64, j] = 1.0
    Rt = R.T.copy()
    return maskAB, identF, identB, Rt


_BUILT = {}


def _get_nc():
    if "nc" not in _BUILT:
        _BUILT["nc"] = Builder().build()
    return _BUILT["nc"]


def kernel(L0_attn_x, L0_attn_wq_w, L0_attn_wq_b, L0_attn_wk_w, L0_attn_wk_b,
           L0_attn_wv_w, L0_attn_wv_b, L0_attn_o_w, L0_attn_o_b,
           L0_ffn_router_w, L0_ffn_gate_w, L0_ffn_up_w, L0_ffn_down_w,
           L0_ffn_s_gate_w, L0_ffn_s_gate_b, L0_ffn_s_up_w, L0_ffn_s_up_b,
           L0_ffn_s_down_w, L0_ffn_s_down_b):
    nc = _get_nc()

    x = np.ascontiguousarray(np.asarray(L0_attn_x, dtype=np.float32))
    router_wT = np.ascontiguousarray(
        np.asarray(L0_ffn_router_w, dtype=np.float32).T)
    pos_all = np.arange(T, dtype=np.float32)[None, :]
    jmod = (np.arange(128, dtype=np.float32) % 64)[:, None]
    common = {
        "wq_w": np.asarray(L0_attn_wq_w, np.float32),
        "wq_b": np.asarray(L0_attn_wq_b, np.float32),
        "wk_w": np.asarray(L0_attn_wk_w, np.float32),
        "wk_b": np.asarray(L0_attn_wk_b, np.float32),
        "wv_w": np.asarray(L0_attn_wv_w, np.float32),
        "wv_b": np.asarray(L0_attn_wv_b, np.float32),
        "o_w": np.asarray(L0_attn_o_w, np.float32),
        "o_b": np.asarray(L0_attn_o_b, np.float32),
        "router_wT": router_wT,
        "gate_w": np.asarray(L0_ffn_gate_w, np.float32),
        "up_w": np.asarray(L0_ffn_up_w, np.float32),
        "down_w": np.asarray(L0_ffn_down_w, np.float32),
        "s_gate_w": np.asarray(L0_ffn_s_gate_w, np.float32),
        "s_gate_b": np.asarray(L0_ffn_s_gate_b, np.float32),
        "s_up_w": np.asarray(L0_ffn_s_up_w, np.float32),
        "s_up_b": np.asarray(L0_ffn_s_up_b, np.float32),
        "s_down_w": np.asarray(L0_ffn_s_down_w, np.float32),
        "s_down_b": np.asarray(L0_ffn_s_down_b, np.float32),
        "pos_all": pos_all,
        "jmod": jmod,
    }
    in_maps = []
    for c in range(8):
        bb, par = c // 2, c % 2
        xb = x[bb]
        xr = xb.reshape(32, 128, D)
        x_own = np.ascontiguousarray(xr[par::2].reshape(TOWN, D))
        pos_own = (np.arange(32).reshape(32)[par::2][:, None] * 128
                   + np.arange(128)[None, :]).reshape(1, TOWN).astype(
                       np.float32)
        maskAB, identF, identB, Rt = _host_consts(par)
        m = dict(common)
        m.update({"x_all": np.ascontiguousarray(xb), "x_own": x_own,
                  "pos_own": pos_own, "maskAB": maskAB, "identF": identF,
                  "identB": identB, "Rt": Rt})
        in_maps.append(m)
    res = run_bass_kernel_spmd(nc, in_maps, core_ids=list(range(8)))
    out = np.empty((4, T, D), dtype=np.float32)
    for c in range(8):
        bb, par = c // 2, c % 2
        zc = np.asarray(res.results[c]["z"], dtype=np.float32)
        out[bb].reshape(32, 128, D)[par::2] = zc.reshape(16, 128, D)
    return out
